# version 01
# baseline (speedup 1.0000x reference)
import numpy as np
from contextlib import ExitStack
import concourse.bass as bass
import concourse.mybir as mybir
from concourse.bass_utils import run_bass_kernel_spmd

F32 = mybir.dt.float32
BF16 = mybir.dt.bfloat16
AF = mybir.ActivationFunctionType
ALU = mybir.AluOpType
AX = mybir.AxisListType

T = 2048
C = 1024
TT = 16
CH = 8
NCORES = 8

N_SUB = 4
FFN_TRUNC = 99
FFN_QP = 99
B_PAIRS = 99
B_STAGE = 99
SKIP12 = False

ENGS = ("pe", "dve", "act", "pool", "sync")


class Sched:

    DMA_SEMS = ("g0", "g1", "g2", "g3", "s0")

    def __init__(self):
        self.q = {e: [] for e in ENGS}
        self.cnt = dict.fromkeys(
            ("pe", "dve", "act", "pool", "sync") + self.DMA_SEMS, 0)
        self.hw = {e: {} for e in ENGS}

    def emit(self, eng, fn, waits=(), inc=True, dma=None):
        if dma:
            sem = dma
            step = 16
            inc = True
        else:
            sem = eng
            step = 1
        wl = []
        for t in waits:
            if t is None:
                continue
            s, v = t
            if self.hw[eng].get(s, 0) < v:
                self.hw[eng][s] = v
                wl.append((s, v))
        tok = None
        if inc:
            self.cnt[sem] += step
            tok = (sem, self.cnt[sem])
        self.q[eng].append((wl, fn, sem if inc else None, step))
        return tok

    def barrier_tokens(self):
        return [(s, v) for s, v in self.cnt.items() if v > 0]

    def check_deadlock(self):
        idx = {e: 0 for e in ENGS}
        sem = dict.fromkeys(self.cnt, 0)
        progress = True
        while progress:
            progress = False
            for e in ENGS:
                while idx[e] < len(self.q[e]):
                    wl, fn, semname, step = self.q[e][idx[e]]
                    if all(sem[s] >= v for s, v in wl):
                        if semname is not None:
                            sem[semname] += step
                        idx[e] += 1
                        progress = True
                    else:
                        break
        stuck = {e: idx[e] for e in ENGS if idx[e] < len(self.q[e])}
        if stuck:
            lines = []
            for e, i in stuck.items():
                wl = self.q[e][i][0]
                blocked = [(s, v, sem[s]) for s, v in wl if sem[s] < v]
                lines.append(f"{e}[{i}/{len(self.q[e])}] blocked on {blocked}")
            raise RuntimeError("DEADLOCK:\n" + "\n".join(lines))

    def run(self, blk, sems):
        self.check_deadlock()
        for eng, meth in (("pe", "tensor"), ("dve", "vector"), ("act", "scalar"),
                          ("pool", "gpsimd"), ("sync", "sync")):
            ops = self.q[eng]
            if not ops:
                continue

            def body(h, ops=ops, sems=sems):
                for wl, fn, sem, step in ops:
                    for s, v in wl:
                        h.wait_ge(sems[s], v)
                    ins = fn(h)
                    if sem is not None:
                        assert ins is not None, "inc'd op returned no instruction"
                        ins.then_inc(sems[sem], step)

            getattr(blk, meth)(body)


class H:
    pass


def _ln_stats(sc, h, tiles, dep, slot=0):
    t_sum = None
    t_sq = None
    for tt in tiles:
        stile = h.s[:, tt, :]
        d = dep[tt] if isinstance(dep, dict) else dep
        t_sum = sc.emit("dve", lambda v, st=stile, tt=tt: v.reduce_sum(
            h.nsum[:, tt:tt + 1], st, axis=AX.X, negate=True), waits=[d])
        t_sq = sc.emit("act", lambda a, st=stile, tt=tt: a.activation(
            h.dump[:, 0:1024], st, AF.Square, bias=0.0, scale=1.0,
            accum_out=h.sumsq[:, tt:tt + 1]), waits=[d])
    lo, hi = min(tiles), min(tiles) + len(tiles)
    t1 = sc.emit("dve", lambda v: v.tensor_scalar_mul(h.nm[:, lo:hi], h.nsum[:, lo:hi], 1.0 / 1024.0),
                 waits=[t_sum])
    t2 = sc.emit("dve", lambda v: v.tensor_mul(h.tmp16[:, lo:hi], h.nm[:, lo:hi], h.nm[:, lo:hi]),
                 waits=[t1])
    t3 = sc.emit("dve", lambda v: v.tensor_scalar(h.var[:, lo:hi], h.sumsq[:, lo:hi],
                                                  1.0 / 1024.0, None, ALU.mult), waits=[t_sq])
    t4 = sc.emit("dve", lambda v: v.tensor_sub(h.var[:, lo:hi], h.var[:, lo:hi], h.tmp16[:, lo:hi]),
                 waits=[t2, t3])
    t4b = sc.emit("dve", lambda v: v.tensor_scalar_add(h.var[:, lo:hi], h.var[:, lo:hi], 1e-5),
                  waits=[t4])
    t5 = sc.emit("act", lambda a: a.activation(h.sd[:, lo:hi], h.var[:, lo:hi], AF.Sqrt,
                                               bias=0.0, scale=1.0), waits=[t4b])
    t_rr = sc.emit("dve", lambda v: v.reciprocal(h.rr[:, lo:hi], h.sd[:, lo:hi]), waits=[t5])
    t_nmr = sc.emit("dve", lambda v: v.tensor_mul(h.nmr[:, lo:hi], h.nm[:, lo:hi], h.rr[:, lo:hi]),
                    waits=[t_rr])
    return t_rr, t_nmr


def _norm_apply_transpose(sc, h, tiles, t_rr, t_nmr, dep, scale_extra=None,
                          feat_w=None, psT_free=None, extra=()):
    tok_xT = {}
    evac_tok = [psT_free[0] if psT_free else None, psT_free[1] if psT_free else None]
    ybf_tok = [None, None]
    for idx, tt in enumerate(tiles):
        half = idx % 2
        ybf = h.ybf[:, half, :]
        if scale_extra is None:
            sc_ap = h.rr[:, tt:tt + 1]
            b_ap = h.nmr[:, tt:tt + 1]
        else:
            sc_ap = h.rr2[:, tt:tt + 1]
            b_ap = h.nmr2[:, tt:tt + 1]
        d = dep[tt] if isinstance(dep, dict) else dep
        t_y = sc.emit("act", lambda a, ybf=ybf, sc_ap=sc_ap, b_ap=b_ap, tt=tt: a.activation(
            ybf, h.s[:, tt, :], AF.Identity, bias=b_ap, scale=sc_ap),
            waits=[d, t_rr, t_nmr, ybf_tok[half]] + list(extra))
        pst = h.psT[:, half * 1024: half * 1024 + 1024]
        tr_last = None
        for ci in range(CH):
            tr_last = sc.emit("pe", lambda t, pst=pst, ybf=ybf, ci=ci: t.transpose(
                pst[:, ci * 128:(ci + 1) * 128], ybf[:, ci * 128:(ci + 1) * 128], h.eye_bf[:]),
                waits=[t_y, evac_tok[half], h.const_tok], inc=(ci == CH - 1))
        ybf_tok[half] = tr_last
        if feat_w is None:
            tok = sc.emit("dve", lambda v, pst=pst, tt=tt: v.tensor_copy(
                h.xT[:, :, tt * 128:(tt + 1) * 128], pst.rearrange("p (c f) -> p c f", f=128)),
                waits=[tr_last])
        else:
            tok = None
            for ci in range(CH):
                tok = sc.emit("dve", lambda v, pst=pst, tt=tt, ci=ci: v.tensor_scalar_mul(
                    h.xT[:, ci, tt * 128:(tt + 1) * 128], pst[:, ci * 128:(ci + 1) * 128],
                    feat_w[:, ci:ci + 1]), waits=[tr_last], inc=(ci == CH - 1))
        evac_tok[half] = tok
        tok_xT[tt] = tok
    return tok_xT, evac_tok


def _emit_ffn(sc, h, d, s_tok, ph):
    QTP = 8

    A = h.arena
    awqr = A[:, 0:8192].rearrange("p (c t) -> p c t", t=1024)
    awor = A[:, 8192:16384].rearrange("p (c t) -> p c t", t=1024)
    akrep = A[:, 16384:24576].rearrange("p (g t) -> p g t", t=2048)
    avtok = A[:, 24576:28672].rearrange("p (t f) -> p t f", f=256)
    aPT = A[:, 28672:32768].rearrange("p (k q) -> p k q", q=256)
    aqTp = A[:, 32768:34816].rearrange("p (c t) -> p c t", t=256)
    aoTp = A[:, 34816:36864].rearrange("p (c t) -> p c t", t=256)
    aspp = A[:, 36864:37888]
    aah = A[:, 37888:39936].rearrange("p (t f) -> p t f", f=1024)
    akv = [A[:, 39936 + i * 2048: 39936 + (i + 1) * 2048].rearrange(
        "p (c t) -> p c t", t=256) for i in range(2)]
    agate = A[:, 44032:44096].rearrange("p (c t) -> p c t", t=8)
    aexp = [A[:, i * 8192:(i + 1) * 8192].rearrange("p (c t) -> p c t", t=512)
            for i in range(2)]

    pe0, dve0, act0 = ph["pe"], ph["dve"], ph["act"]

    def stream(dst3, src, ncin=CH, waits=()):
        t = None
        for cin in range(ncin):
            t = sc.emit("pool", lambda g, cin=cin: g.dma_start(
                out=dst3[:, cin, :], in_=src[cin * 128:(cin + 1) * 128, :]),
                waits=list(waits), dma="g2")
        return t

    t_wq = stream(awqr, d["fwq"], waits=[pe0, dve0, act0])
    t_wo = stream(awor, d["fwo"], waits=[t_wq])
    t_wk = stream(akv[0], d["fwk"], waits=[t_wo])
    t_wv = stream(akv[1], d["fwv"], waits=[t_wk])
    t_gate = stream(agate, d["fgate"], waits=[t_wv])

    t_rr, t_nmr = _ln_stats(sc, h, list(range(TT)), s_tok)
    tw1 = sc.emit("dve", lambda v: v.reciprocal(h.w16a[:], h.var[:]), waits=[t_rr])
    tw2 = sc.emit("dve", lambda v: v.tensor_scalar(
        h.w16a[:], h.w16a[:], -1e-5, 1.0 + 1e-6, ALU.mult, ALU.add), waits=[tw1])
    tw3 = sc.emit("act", lambda a: a.activation(h.w16b[:], h.w16a[:], AF.Sqrt,
                                                bias=0.0, scale=1.0), waits=[tw2])
    tw4 = sc.emit("dve", lambda v: v.reciprocal(h.w16a[:], h.w16b[:]), waits=[tw3])
    t_rho = sc.emit("dve", lambda v: v.tensor_mul(h.rr2[:], h.rr[:], h.w16a[:]),
                    waits=[tw4])
    t_nmr2 = sc.emit("dve", lambda v: v.tensor_mul(h.nmr2[:], h.nm[:], h.rr2[:]),
                     waits=[t_rho])

    tok_xT, psT_free = _norm_apply_transpose(
        sc, h, list(range(TT)), t_rho, t_nmr2, s_tok, scale_extra=True,
        feat_w=h.fnw16[:, 0:8], psT_free=None, extra=[h.fnw_tok, pe0, dve0])

    if FFN_TRUNC < 2:
        return {"pe": ("pe", sc.cnt["pe"]), "dve": ("dve", sc.cnt["dve"]),
                "act": ("act", sc.cnt["act"])}
    psA_free = [None, None, None, None]
    kr_tok = None
    sl = 0
    for g in range(4):
        for ts in range(4):
            slot = sl % 4
            sl += 1
            ps = h.psA[:, slot * 512: slot * 512 + 512]
            mm = None
            for cin in range(CH):
                for rep in range(2):
                    mm = sc.emit("pe", lambda t, ps=ps, g=g, ts=ts, cin=cin, rep=rep: t.matmul(
                        ps[rep * 64:rep * 64 + 64, :],
                        akv[0][:, cin, g * 64:(g + 1) * 64],
                        h.xT[:, cin, ts * 512:(ts + 1) * 512],
                        start=(cin == 0), stop=(cin == CH - 1),
                        tile_position=(0, rep * 64)),
                        waits=[t_wk, tok_xT[ts * 4 + 3], psA_free[slot]],
                        inc=(cin == CH - 1 and rep == 1))
            ev = sc.emit("dve", lambda v, ps=ps, g=g, ts=ts: v.tensor_copy(
                akrep[:, g, ts * 512:(ts + 1) * 512], ps[:]), waits=[mm])
            psA_free[slot] = ev
            kr_tok = ev
    tok_psB = None
    tok_psC = None
    vt_tok = None
    for tt in range(TT):
        ps = h.psB if tt % 2 == 0 else h.psC
        ptok = tok_psB if tt % 2 == 0 else tok_psC
        mm = None
        for cin in range(CH):
            mm = sc.emit("pe", lambda t, ps=ps, tt=tt, cin=cin: t.matmul(
                ps[:, 0:256], h.xT[:, cin, tt * 128:(tt + 1) * 128],
                akv[1][:, cin, :], start=(cin == 0), stop=(cin == CH - 1)),
                waits=[t_wv, tok_xT[tt], ptok], inc=(cin == CH - 1))
        eng = "dve" if tt % 2 == 0 else "act"
        if eng == "dve":
            ev = sc.emit("dve", lambda v, ps=ps, tt=tt: v.tensor_copy(
                avtok[:, tt, :], ps[:, 0:256]), waits=[mm])
        else:
            ev = sc.emit("act", lambda a, ps=ps, tt=tt: a.activation(
                avtok[:, tt, :], ps[:, 0:256], AF.Identity, bias=0.0, scale=1.0),
                waits=[mm])
        if tt % 2 == 0:
            tok_psB = ev
        else:
            tok_psC = ev
        vt_tok = ev

    if FFN_TRUNC < 3:
        return {"pe": ("pe", sc.cnt["pe"]), "dve": ("dve", sc.cnt["dve"]),
                "act": ("act", sc.cnt["act"])}
    small_ctr = 0
    slot_free = {}
    regA_free = [kr_tok, kr_tok]
    spp_tok = None
    pt_tok = None
    qT_tok = None
    qT_pe_done = None
    wo_mm_done = None
    hT_tok = {}
    comb_tok = {}
    prev_t9 = None
    prev_t12 = None
    prev_sq2 = [None, None]
    prev_trh = [None, None]
    s2_tok = dict(s_tok)
    rega = 0
    for qp in range(min(QTP, FFN_QP)):
        tts = (2 * qp, 2 * qp + 1)
        for pair in range(8):
            ps = h.psB if pair % 2 == 0 else h.psC
            ptok = tok_psB if pair % 2 == 0 else tok_psC
            mm = None
            for cin in range(CH):
                mm = sc.emit("pe", lambda t, ps=ps, pair=pair, cin=cin, qp=qp: t.matmul(
                    ps[:, 0:256], awqr[:, cin, pair * 128:(pair + 1) * 128],
                    h.xT[:, cin, qp * 256:(qp + 1) * 256],
                    start=(cin == 0), stop=(cin == CH - 1)),
                    waits=[t_wq, tok_xT[2 * qp + 1], ptok], inc=(cin == CH - 1))
            ev = sc.emit("act", lambda a, ps=ps, pair=pair: a.activation(
                aqTp[:, pair, :], ps[:, 0:256], AF.Identity, bias=0.0, scale=0.125),
                waits=[mm, qT_tok, pt_tok])
            if pair % 2 == 0:
                tok_psB = ev
            else:
                tok_psC = ev
            if pair == 7:
                qT_tok = ev
                qT_pe_done = mm
        for pair in range(8):
            g = pair // 2
            pv_done = None
            for h2 in range(2):
                rb = h2 * 64
                for ql in range(2):
                    qt = 2 * qp + ql
                    jlen = (qt + 1) * 128
                    subs = []
                    j0 = 0
                    while j0 < jlen:
                        subs.append((j0, min(jlen - j0, 1024)))
                        j0 += 1024
                    nsub = len(subs)
                    sl_ids = []
                    sub_toks = []
                    regs = []
                    for si, (j0, jn) in enumerate(subs):
                        r = rega % 2
                        rega += 1
                        regs.append(r)
                        ps = h.psA[:, r * 1024: r * 1024 + jn]
                        mm = None
                        nslab = (jn + 511) // 512
                        has_diag = (j0 <= jlen - 128 < j0 + jn)
                        for ks in range(nslab):
                            kn = min(512, jn - ks * 512)
                            is_dslab = has_diag and (ks == nslab - 1)
                            mm = sc.emit("pe", lambda t, ps=ps, pair=pair, rb=rb, g=g,
                                         qt=qt, j0=j0, ks=ks, kn=kn, is_dslab=is_dslab: t.matmul(
                                ps[:, ks * 512:ks * 512 + kn],
                                aqTp[rb:rb + 64, pair, (qt % 2) * 128:(qt % 2) * 128 + 128],
                                akrep[rb:rb + 64, g, j0 + ks * 512:j0 + ks * 512 + kn],
                                start=True, stop=not is_dslab, tile_position=(rb, 0)),
                                waits=[qT_tok, kr_tok, regA_free[r]],
                                inc=(not has_diag and ks == nslab - 1))
                        if has_diag:
                            doff = qt * 128 - j0
                            mm = sc.emit("pe", lambda t, ps=ps, doff=doff: t.matmul(
                                ps[:, doff:doff + 128], h.eye_bf[:], h.mneg[:],
                                start=False, stop=True), waits=[h.const_tok])
                        sl_i = small_ctr % 12
                        small_ctr += 1
                        sl_ids.append(sl_i)
                        nm_ap = h.small[:, sl_i * 4: sl_i * 4 + 1]
                        z_ap = h.small[:, sl_i * 4 + 1: sl_i * 4 + 2]
                        t_nm = sc.emit("dve", lambda v, ps=ps, nm_ap=nm_ap: v.reduce_max(
                            nm_ap, ps[:], axis=AX.X, negate=True),
                            waits=[mm] + slot_free.get(sl_i, []))
                        t_z = sc.emit("act", lambda a, ps=ps, nm_ap=nm_ap, z_ap=z_ap, jn=jn:
                                      a.activation(h.dump[:, 0:jn], ps[:], AF.Exp,
                                                   bias=nm_ap, scale=1.0, accum_out=z_ap),
                                      waits=[t_nm])
                        sub_toks.append((t_nm, t_z, nm_ap, z_ap))
                    sl0 = sl_ids[0]
                    nm0 = h.small[:, sl0 * 4: sl0 * 4 + 1]
                    z0 = h.small[:, sl0 * 4 + 1: sl0 * 4 + 2]
                    lz = h.small[:, sl0 * 4 + 2: sl0 * 4 + 3]
                    b2 = h.small[:, sl0 * 4 + 3: sl0 * 4 + 4]
                    if nsub == 1:
                        t_lz = sc.emit("act", lambda a, z0=z0, lz=lz: a.activation(
                            lz, z0, AF.Ln, bias=0.0, scale=1.0), waits=[sub_toks[0][1]])
                        t_b2 = sc.emit("dve", lambda v, nm0=nm0, lz=lz, b2=b2: v.tensor_sub(
                            b2, nm0, lz), waits=[t_lz])
                    else:
                        sl1 = sl_ids[1]
                        nm1 = h.small[:, sl1 * 4: sl1 * 4 + 1]
                        z1 = h.small[:, sl1 * 4 + 1: sl1 * 4 + 2]
                        d0 = h.small[:, sl1 * 4 + 2: sl1 * 4 + 3]
                        d1 = h.small[:, sl1 * 4 + 3: sl1 * 4 + 4]
                        tmin = sc.emit("dve", lambda v, nm0=nm0, nm1=nm1, d0=d0: v.tensor_tensor(
                            d0, nm0, nm1, ALU.min), waits=[sub_toks[0][0], sub_toks[1][0]])
                        t1 = sc.emit("dve", lambda v, d0=d0, nm0=nm0, d1=d1: v.tensor_sub(
                            d1, d0, nm0), waits=[tmin])
                        te1 = sc.emit("act", lambda a, d1=d1: a.activation(
                            d1, d1, AF.Exp, bias=0.0, scale=1.0), waits=[t1])
                        t2 = sc.emit("dve", lambda v, z0=z0, d1=d1: v.tensor_mul(
                            z0, z0, d1), waits=[te1, sub_toks[0][1]])
                        t3 = sc.emit("dve", lambda v, d1=d1, d0=d0, nm1=nm1: v.tensor_sub(
                            d1, d0, nm1), waits=[t2])
                        te2 = sc.emit("act", lambda a, d1=d1: a.activation(
                            d1, d1, AF.Exp, bias=0.0, scale=1.0), waits=[t3])
                        t4 = sc.emit("dve", lambda v, z1=z1, d1=d1: v.tensor_mul(
                            z1, z1, d1), waits=[te2, sub_toks[1][1]])
                        t5 = sc.emit("dve", lambda v, z0=z0, z1=z1: v.tensor_add(
                            z0, z0, z1), waits=[t4])
                        tnm = sc.emit("dve", lambda v, nm0=nm0, d0=d0: v.tensor_copy(
                            nm0, d0), waits=[tmin])
                        t_lz = sc.emit("act", lambda a, z0=z0, lz=lz: a.activation(
                            lz, z0, AF.Ln, bias=0.0, scale=1.0), waits=[t5])
                        t_b2 = sc.emit("dve", lambda v, nm0=nm0, lz=lz, b2=b2: v.tensor_sub(
                            b2, nm0, lz), waits=[t_lz, tnm])
                        slot_free[sl1] = [t_b2]
                    slot_free[sl0] = [t_b2]
                    for si, (j0, jn) in enumerate(subs):
                        r = regs[si]
                        ps = h.psA[:, r * 1024: r * 1024 + jn]
                        t_sp = sc.emit("dve", lambda v, ps=ps, jn=jn, b2=b2: v.tensor_scalar_add(
                            aspp[:, 0:jn], ps[:], b2), waits=[t_b2, spp_tok])
                        regA_free[r] = t_sp
                        slot_free[sl_ids[si]] = slot_free.get(sl_ids[si], []) + [t_sp]
                        nkt = jn // 128
                        tq = rega % 2
                        pst = h.psT[:, tq * 1024: tq * 1024 + jn]
                        tr = None
                        for kt in range(nkt):
                            tr = sc.emit("pe", lambda t, pst=pst, kt=kt: t.transpose(
                                pst[:, kt * 128:(kt + 1) * 128], aspp[:, kt * 128:(kt + 1) * 128],
                                h.eye_bf[:]), waits=[t_sp, psT_free[tq]], inc=(kt == nkt - 1))
                        spp_tok = tr
                        t_pt = sc.emit("act", lambda a, pst=pst, j0=j0, nkt=nkt, ql=ql: a.activation(
                            aPT[:, j0 // 128: j0 // 128 + nkt, ql * 128:(ql + 1) * 128],
                            pst.rearrange("p (k q) -> p k q", q=128), AF.Exp,
                            bias=0.0, scale=1.0), waits=[tr, pt_tok])
                        psT_free[tq] = t_pt
                nkt_all = 2 * qp + 2
                pv = None
                for kt in range(nkt_all):
                    qs = max(0, (kt - 2 * qp) * 128)
                    pv = sc.emit("pe", lambda t, g=g, rb=rb, kt=kt, qs=qs, nkt_all=nkt_all: t.matmul(
                        h.psB[rb:rb + 64, qs:256], avtok[:, kt, g * 64:(g + 1) * 64],
                        aPT[:, kt, qs:256], start=(kt == 0), stop=(kt == nkt_all - 1),
                        tile_position=(0, rb)),
                        waits=[psT_free[0], psT_free[1], vt_tok, tok_psB],
                        inc=(kt == nkt_all - 1))
                pt_tok = pv
                pv_done = pv
            ev = sc.emit("dve", lambda v, pair=pair: v.tensor_copy(
                aoTp[:, pair, :], h.psB[:, 0:256]), waits=[pv_done, wo_mm_done])
            tok_psB = ev
            oT_tok = ev
        for tl, tt in enumerate(tts):
            if FFN_TRUNC < 4:
                continue
            t_y2 = sc.emit("dve", lambda v, tt=tt, tl=tl: v.tensor_scalar(
                h.f32buf[:, tl * 1024:(tl + 1) * 1024], h.s[:, tt, :],
                h.nm[:, tt:tt + 1], h.rr[:, tt:tt + 1], ALU.add, ALU.mult),
                waits=[t_rr, s2_tok[tt]])
            t_h = [None, None]
            for hf in range(2):
                mm = None
                for cin in range(CH):
                    mm = sc.emit("pe", lambda t, cin=cin, tl=tl, hf=hf: t.matmul(
                        h.psC[:], aoTp[:, cin, tl * 128:(tl + 1) * 128],
                        awor[:, cin, hf * 512:hf * 512 + 512],
                        start=(cin == 0), stop=(cin == CH - 1)),
                        waits=[t_wo, oT_tok, tok_psC], inc=(cin == CH - 1))
                wo_mm_done = mm
                t_h[hf] = sc.emit("dve", lambda v, tl=tl, hf=hf: v.tensor_add(
                    aah[:, tl, hf * 512:hf * 512 + 512],
                    h.f32buf[:, tl * 1024 + hf * 512: tl * 1024 + hf * 512 + 512],
                    h.psC[:]), waits=[mm, t_y2, prev_sq2[tl], prev_trh[tl]])
                tok_psC = t_h[hf]
                t_s = sc.emit("dve", lambda v, tl=tl, tt=tt, hf=hf: v.tensor_add(
                    h.s[:, tt, hf * 512:hf * 512 + 512],
                    h.s[:, tt, hf * 512:hf * 512 + 512],
                    aah[:, tl, hf * 512:hf * 512 + 512]), waits=[t_h[hf]])
                s2_tok[tt] = t_s
            t_sq2 = sc.emit("act", lambda a, tl=tl, tt=tt: a.activation(
                h.dump[:, 0:1024], aah[:, tl, :], AF.Square, bias=0.0, scale=1.0,
                accum_out=h.msq2[:, tt:tt + 1]), waits=[t_h[0], t_h[1]])
            prev_sq2[tl] = t_sq2
            tm1 = sc.emit("dve", lambda v, tt=tt: v.tensor_scalar(
                h.rq2[:, tt:tt + 1], h.msq2[:, tt:tt + 1], 1.0 / 1024.0, 1e-6,
                ALU.mult, ALU.add), waits=[t_sq2])
            tm2 = sc.emit("act", lambda a, tt=tt: a.activation(
                h.w16b[:, tt:tt + 1], h.rq2[:, tt:tt + 1], AF.Sqrt, bias=0.0,
                scale=1.0), waits=[tm1])
            t_rho2 = sc.emit("dve", lambda v, tt=tt: v.reciprocal(
                h.rq2[:, tt:tt + 1], h.w16b[:, tt:tt + 1]), waits=[tm2])
            pstq = h.psT[:, 0:1024]
            tr = None
            for ci in range(CH):
                tr = sc.emit("pe", lambda t, pstq=pstq, ci=ci, tl=tl: t.transpose(
                    pstq[:, ci * 128:(ci + 1) * 128], aah[:, tl, ci * 128:(ci + 1) * 128],
                    h.eye_bf[:]), waits=[t_h[0], t_h[1], qT_pe_done, psT_free[0]],
                    inc=(ci == CH - 1))
            prev_trh[tl] = tr
            tok = None
            for ci in range(CH):
                tok = sc.emit("dve", lambda v, pstq=pstq, tt=tt, ci=ci: v.tensor_scalar_mul(
                    h.xT[:, ci, tt * 128:(tt + 1) * 128], pstq[:, ci * 128:(ci + 1) * 128],
                    h.fnw16[:, 8 + ci:9 + ci]), waits=[tr, h.fnw_tok], inc=(ci == CH - 1))
            psT_free[0] = tok
            hT_tok[tt] = tok
            gmm = None
            for cin in range(CH):
                gmm = sc.emit("pe", lambda t, cin=cin, tl=tl, tt=tt: t.matmul(
                    h.psC[:, 0:8], h.xT[:, cin, tt * 128:(tt + 1) * 128],
                    agate[:, cin, :], start=(cin == 0), stop=(cin == CH - 1)),
                    waits=[t_gate, tok, tok_psC], inc=(cin == CH - 1))
            cm = h.comb[:, tt, :]
            nm1 = h.small[:, 48:49]
            pm2 = h.small[:, 49:50]
            iz = h.small[:, 50:51]
            glp = h.small[:, 52:60]
            gl2 = h.small[:, 64:72]
            t_gl = sc.emit("dve", lambda v, cm=cm, tl=tl, tt=tt: v.tensor_scalar(
                cm, h.psC[:, 0:8], h.rq2[:, tt:tt + 1], None, ALU.mult),
                waits=[gmm, t_rho2, comb_tok.get(tt)])
            tok_psC = t_gl
            t1 = sc.emit("dve", lambda v, cm=cm, nm1=nm1: v.reduce_max(
                nm1, cm, axis=AX.X, negate=True), waits=[t_gl, prev_t9, prev_t12])
            t2 = sc.emit("dve", lambda v, cm=cm, nm1=nm1: v.tensor_scalar_add(
                cm, cm, nm1), waits=[t1])
            t3 = sc.emit("dve", lambda v, cm=cm, glp=glp: v.tensor_scalar(
                glp, cm, 0.0, None, ALU.is_ge), waits=[t2])
            t4 = sc.emit("dve", lambda v, cm=cm, glp=glp, gl2=gl2: v.scalar_tensor_tensor(
                gl2, glp, -1e30, cm, ALU.mult, ALU.add), waits=[t3])
            t5 = sc.emit("dve", lambda v, gl2=gl2, pm2=pm2: v.reduce_max(
                pm2, gl2, axis=AX.X, negate=True), waits=[t4])
            t6 = sc.emit("dve", lambda v, pm2=pm2: v.tensor_scalar_mul(
                pm2, pm2, -1.0), waits=[t5])
            t7 = sc.emit("dve", lambda v, cm=cm, glp=glp, pm2=pm2: v.tensor_scalar(
                glp, cm, pm2, None, ALU.is_ge), waits=[t6])
            t8 = sc.emit("act", lambda a, cm=cm: a.activation(
                cm, cm, AF.Exp, bias=0.0, scale=1.0), waits=[t7])
            t9 = sc.emit("act", lambda a, pm2=pm2, iz=iz: a.activation(
                iz, pm2, AF.Exp, bias=0.0, scale=1.0), waits=[t6])
            prev_t9 = t9
            t10 = sc.emit("dve", lambda v, iz=iz: v.tensor_scalar_add(
                iz, iz, 1.0), waits=[t9])
            t11 = sc.emit("dve", lambda v, iz=iz: v.reciprocal(iz, iz), waits=[t10])
            t12 = sc.emit("dve", lambda v, cm=cm, iz=iz, glp=glp: v.scalar_tensor_tensor(
                cm, cm, iz, glp, ALU.mult, ALU.mult), waits=[t8, t11])
            prev_t12 = t12
            comb_tok[tt] = t12

    if FFN_TRUNC < 5:
        for tt in range(TT):
            s_tok[tt] = s2_tok[tt]
        return {"pe": ("pe", sc.cnt["pe"]), "dve": ("dve", sc.cnt["dve"]),
                "act": ("act", sc.cnt["act"])}
    att_pe_done = ("pe", sc.cnt["pe"])
    eb_tok = [None, None]
    eb_free = [att_pe_done, att_pe_done]
    last_hT = hT_tok[TT - 1]
    for e in range(8):
        bi = e % 2
        t_e = None
        for ci2 in range(16):
            t_e = sc.emit("pool", lambda g, e=e, ci2=ci2, bi=bi: g.dma_start(
                out=aexp[bi][:, ci2, :],
                in_=d["fexp"][e, (ci2 // 2) * 128:(ci2 // 2) * 128 + 128,
                              (ci2 % 2) * 512:(ci2 % 2) * 512 + 512]),
                waits=[eb_free[bi]], dma="g0" if bi == 0 else "g1")
        eb_tok[bi] = t_e
        mm_last = None
        for tt in range(TT):
            for hf in range(2):
                ps = h.psB if hf == 0 else h.psC
                ptok = tok_psB if hf == 0 else tok_psC
                mm = None
                for cin in range(CH):
                    mm = sc.emit("pe", lambda t, ps=ps, bi=bi, tt=tt, cin=cin, hf=hf: t.matmul(
                        ps[:], h.xT[:, cin, tt * 128:(tt + 1) * 128],
                        aexp[bi][:, cin * 2 + hf, 0:512],
                        start=(cin == 0), stop=(cin == CH - 1)),
                        waits=[eb_tok[bi], last_hT, ptok], inc=(cin == CH - 1))
                t_s = sc.emit("dve", lambda v, ps=ps, tt=tt, hf=hf, e=e: v.scalar_tensor_tensor(
                    h.s[:, tt, hf * 512:hf * 512 + 512], ps[:],
                    h.comb[:, tt, e:e + 1], h.s[:, tt, hf * 512:hf * 512 + 512],
                    ALU.mult, ALU.add), waits=[mm, comb_tok[tt], s2_tok[tt]])
                if hf == 0:
                    tok_psB = t_s
                else:
                    tok_psC = t_s
                s2_tok[tt] = t_s
                mm_last = mm
        eb_free[bi] = mm_last

    for tt in range(TT):
        s_tok[tt] = s2_tok[tt]
    return {"pe": ("pe", sc.cnt["pe"]), "dve": ("dve", sc.cnt["dve"]),
            "act": ("act", sc.cnt["act"])}


def _emit_attn_b(sc, h, d, s_tok, ph):
    A = h.arena
    o = [0]

    def carve(n):
        ap = A[:, o[0]:o[0] + n]
        o[0] += n
        return ap

    aOT = carve(16384).rearrange("p (c t) -> p c t", t=2048)
    off_q = o[0]
    aqT = carve(2048)
    akT = carve(2048)
    avtk = carve(2048).rearrange("p (t f) -> p t f", f=128)
    aDn = carve(2048)
    awo_final = A[:, off_q:off_q + 8192].rearrange("p (c t) -> p c t", t=1024)
    awtok = carve(2048).rearrange("p (t f) -> p t f", f=128)
    awng = carve(512).rearrange("p (l f) -> p l f", f=128)
    abwt = carve(512).rearrange("p (l f) -> p l f", f=128)
    abwn = carve(512).rearrange("p (l f) -> p l f", f=128)
    aC = carve(8192).rearrange("p (l t) -> p l t", t=2048)
    aspp = h.dump[:, 0:1024]
    ainv = carve(512).rearrange("p (l f) -> p l f", f=128)
    nb0 = o[0]
    aNw = [carve(128) for _ in range(9)]
    nU, nV, nAT, nXa, nXTa, nXb, nXTb, nY, nSp = aNw
    nATf = A[:, nb0 + 7 * 128: nb0 + 9 * 128].bitcast(F32)
    nXf = A[:, nb0 + 5 * 128: nb0 + 7 * 128].bitcast(F32)
    nYf = A[:, nb0 + 0 * 128: nb0 + 2 * 128].bitcast(F32)
    nXTf = A[:, nb0 + 0 * 128: nb0 + 2 * 128].bitcast(F32)
    nWf = A[:, nb0 + 5 * 128: nb0 + 7 * 128].bitcast(F32)
    aWT = aNw[0]
    aLhat = [aNw[1], aNw[2], aNw[3]]
    aLic = aNw[4]
    aSLWK = aNw[5]
    aGht = [aNw[6], aNw[7], aNw[8], aNw[4]]
    awq = carve(1024).rearrange("p (c t) -> p c t", t=128)
    awk = carve(1024).rearrange("p (c t) -> p c t", t=128)
    awv = carve(1024).rearrange("p (c t) -> p c t", t=128)
    aw1 = carve(512).rearrange("p (c t) -> p c t", t=64)
    aw2 = carve(1024)
    awbf = carve(256).rearrange("p (c t) -> p c t", t=32)
    auT = carve(2048)
    assert o[0] <= 44928, o[0]
    aPT = h.ybf[:].rearrange("p a f -> p (a f)").rearrange("p (k q) -> p k q", q=128)
    FcT = h.f32buf[64:80, :]
    LFt = h.f32buf[32:48, :]
    fcrow = h.f32buf[0:1, :]

    pe0, dve0, act0 = ph["pe"], ph["dve"], ph["act"]

    def pe_now():
        return ("pe", sc.cnt["pe"])

    t_rr, t_nmr = _ln_stats(sc, h, list(range(TT)), s_tok)
    tok_xT, psT_free = _norm_apply_transpose(
        sc, h, list(range(TT)), t_rr, t_nmr, s_tok, extra=[pe0, dve0])
    xT_all = tok_xT[TT - 1]
    pst1_tok = psT_free[1]

    def stream(dst3, src, waits=(), sem="g2"):
        t = None
        for cin in range(CH):
            t = sc.emit("pool", lambda g, cin=cin: g.dma_start(
                out=dst3[:, cin, :], in_=src[cin * 128:(cin + 1) * 128, :]),
                waits=list(waits), dma=sem)
        return t

    t_w1 = stream(aw1, d["bww1"], waits=[pe0, dve0, act0])
    t_w2 = sc.emit("pool", lambda g: g.dma_start(out=aw2[0:64, :], in_=d["bww2"]),
                   waits=[t_w1], dma="g2")
    t_wbf = stream(awbf, d["bwbf"], waits=[t_w2])
    t_dj = sc.emit("pool", lambda g: g.dma_start(
        out=h.small[0:1, 80:96], in_=d["bdelta"]), waits=[t_wbf], dma="g2")

    psA_free = [None, None, None, None]
    t_uT = None
    for ts in range(4):
        ps = h.psA[:, (ts % 4) * 512:(ts % 4) * 512 + 512]
        mm = None
        for cin in range(CH):
            mm = sc.emit("pe", lambda t, ps=ps, ts=ts, cin=cin: t.matmul(
                ps[0:64, :], aw1[:, cin, :], h.xT[:, cin, ts * 512:(ts + 1) * 512],
                start=(cin == 0), stop=(cin == CH - 1)),
                waits=[t_w1, tok_xT[ts * 4 + 3], psA_free[ts % 4], pe0],
                inc=(cin == CH - 1))
        t_uT = sc.emit("dve", lambda v, ps=ps, ts=ts: v.tensor_copy(
            auT[0:64, ts * 512:(ts + 1) * 512], ps[0:64, :]), waits=[mm])
        psA_free[ts % 4] = t_uT

    t_db = sc.emit("pe", lambda t: t.matmul(
        h.psC[:, 0:16], h.onesr[:], h.small[0:1, 80:96],
        start=True, stop=True), waits=[t_dj, h.const_tok])
    t_db2 = sc.emit("dve", lambda v: v.tensor_copy(h.small[:, 80:96], h.psC[:, 0:16]),
                    waits=[t_db])
    bl_tok = None
    t_beta = None
    t_b = None
    for tt in range(TT):
        mm = None
        for cin in range(CH):
            mm = sc.emit("pe", lambda t, tt=tt, cin=cin: t.matmul(
                h.psB[:, 0:32], h.xT[:, cin, tt * 128:(tt + 1) * 128], awbf[:, cin, :],
                start=(cin == 0), stop=(cin == CH - 1)),
                waits=[t_wbf, tok_xT[tt], bl_tok, t_b, t_db2], inc=(cin == CH - 1))
        t_b = sc.emit("act", lambda a, tt=tt: a.activation(
            h.beta[:, tt, :], h.psB[:, 0:16], AF.Sigmoid,
            bias=0.0, scale=1.0), waits=[mm, t_beta])
        t_beta = sc.emit("dve", lambda v, tt=tt: v.tensor_scalar_mul(
            h.beta[:, tt, :], h.beta[:, tt, :], 2.0), waits=[t_b])
        t_zz = sc.emit("dve", lambda v, tt=tt: v.tensor_add(
            h.blf[:, tt * 16: tt * 16 + 16], h.psB[:, 16:32],
            h.small[:, 80:96]), waits=[mm])
        bl_tok = t_zz
    beta_tok = t_beta
    zall = h.blf[:, 0:256]
    tmpb = h.blf[:, 256:512]
    ta1 = sc.emit("act", lambda a: a.activation(tmpb, zall, AF.Abs, bias=0.0,
                                                scale=1.0), waits=[bl_tok])
    ta2 = sc.emit("act", lambda a: a.activation(tmpb, tmpb, AF.Exp, bias=0.0,
                                                scale=-1.0), waits=[ta1])
    ta3 = sc.emit("act", lambda a: a.activation(tmpb, tmpb, AF.Ln, bias=1.0,
                                                scale=1.0), waits=[ta2])
    ta4 = sc.emit("dve", lambda v: v.tensor_scalar_min(zall, zall, 0.0),
                  waits=[bl_tok, ta1])
    ta5 = sc.emit("dve", lambda v: v.tensor_sub(zall, zall, tmpb), waits=[ta3, ta4])
    lf_ev = None
    psC_tok = t_db2
    for tt in range(TT):
        tr = sc.emit("pe", lambda t, tt=tt: t.transpose(
            h.psC[0:16, 128:256], zall[:, tt * 16:(tt + 1) * 16], h.eye_f[:]),
            waits=[ta5, psC_tok, h.const_tok])
        lf_ev = sc.emit("dve", lambda v, tt=tt: v.tensor_copy(
            LFt[:, tt * 128:(tt + 1) * 128], h.psC[0:16, 128:256]), waits=[tr])
        psC_tok = lf_ev
    t_fc = sc.emit("dve", lambda v: v.tensor_tensor_scan(
        FcT[:, :], LFt[:, :], LFt[:, :], 0.0, ALU.add, ALU.bypass), waits=[lf_ev])
    t_fc = sc.emit("dve", lambda v: v.tensor_scalar_mul(FcT[:, :], FcT[:, :], -1.0),
                   waits=[t_fc])

    small_ctr = 0
    slot_free = {}
    regA_free = [psA_free[1], psA_free[3]]
    regA2_free = psA_free
    spp_tok = None
    pt_tok = None
    psB_tok = None
    fc_tok = None
    wst_tok = [None, None, None]
    s2_tok = dict(s_tok)
    rega = 0
    prev_pair_pe = None
    pst0_tok = psT_free[0]
    for pair in range(min(8, B_PAIRS)):
        tws = []
        for wi, (sem, (dst, src)) in enumerate(zip(
                ("g0", "g1", "g3"),
                ((awq, d["bwq"]), (awk, d["bwk"]), (awv, d["bwv"])))):
            t = None
            for cin in range(CH):
                t = sc.emit("pool", lambda g, dst=dst, src=src, cin=cin, pair=pair:
                            g.dma_start(
                    out=dst[:, cin, :],
                    in_=src[cin * 128:(cin + 1) * 128, pair * 128:(pair + 1) * 128]),
                    waits=[wst_tok[wi], pe0, dve0, act0, prev_pair_pe], dma=sem)
            tws.append(t)

        def proj(dst_flat, wsrc, twait, scale, eng):
            ev = None
            for ts in range(4):
                ps = h.psA[:, (ts % 2) * 512:(ts % 2) * 512 + 512]
                mm = None
                for cin in range(CH):
                    mm = sc.emit("pe", lambda t, ps=ps, ts=ts, cin=cin, wsrc=wsrc:
                                 t.matmul(
                        ps[:], wsrc[:, cin, :], h.xT[:, cin, ts * 512:(ts + 1) * 512],
                        start=(cin == 0), stop=(cin == CH - 1)),
                        waits=[twait, xT_all, regA2_free[ts % 2], prev_pair_pe],
                        inc=(cin == CH - 1))
                if eng == "act":
                    ev = sc.emit("act", lambda a, ps=ps, ts=ts, dst_flat=dst_flat,
                                 scale=scale: a.activation(
                        dst_flat[:, ts * 512:(ts + 1) * 512], ps[:], AF.Identity,
                        bias=0.0, scale=scale), waits=[mm])
                else:
                    ev = sc.emit("dve", lambda v, ps=ps, ts=ts, dst_flat=dst_flat:
                                 v.tensor_copy(
                        dst_flat[:, ts * 512:(ts + 1) * 512], ps[:]), waits=[mm])
                regA2_free[ts % 2] = ev
            return ev

        tqv = proj(aqT, awq, tws[0], 0.125, "act")
        tkv = proj(akT, awk, tws[1], 1.0, "dve")

        wt_toks = {}
        for i in range(TT):
            mm = sc.emit("pe", lambda t, i=i, pair=pair: t.matmul(
                h.psB[:, 128:256], auT[0:64, i * 128:(i + 1) * 128],
                aw2[0:64, pair * 128:(pair + 1) * 128], start=True, stop=True),
                waits=[t_w2, t_uT, psB_tok, prev_pair_pe], inc=True)
            ev = sc.emit("dve", lambda v, i=i: v.tensor_copy(
                awtok[:, i, :], h.psB[:, 128:256]), waits=[mm])
            psB_tok = ev
            sq0 = sc.emit("act", lambda a, i=i: a.activation(
                h.dump[:, 128:192], awtok[:, i, 0:64], AF.Square, bias=0.0, scale=1.0,
                accum_out=h.small[:, 76:77]), waits=[ev])
            sq1 = sc.emit("act", lambda a, i=i: a.activation(
                h.dump[:, 192:256], awtok[:, i, 64:128], AF.Square, bias=0.0,
                scale=1.0, accum_out=h.small[:, 77:78]), waits=[sq0, ev])
            st1 = sc.emit("dve", lambda v: v.tensor_scalar_add(
                h.small[:, 76:78], h.small[:, 76:78], 1e-6), waits=[sq0, sq1])
            st2 = sc.emit("act", lambda a: a.activation(
                h.small[:, 78:80], h.small[:, 76:78], AF.Sqrt, bias=0.0, scale=1.0),
                waits=[st1])
            st3 = sc.emit("dve", lambda v: v.reciprocal(
                h.small[:, 76:78], h.small[:, 78:80]), waits=[st2])
            ap0 = sc.emit("dve", lambda v, i=i: v.tensor_scalar_mul(
                awtok[:, i, 0:64], awtok[:, i, 0:64], h.small[:, 76:77]), waits=[st3])
            ap1 = sc.emit("dve", lambda v, i=i: v.tensor_scalar_mul(
                awtok[:, i, 64:128], awtok[:, i, 64:128], h.small[:, 77:78]),
                waits=[st3])
            wt_toks[i] = (ap0, ap1)
        vt_ev = None
        for tt in range(TT):
            mm = None
            for cin in range(CH):
                mm = sc.emit("pe", lambda t, tt=tt, cin=cin: t.matmul(
                    h.psB[:, 0:128], h.xT[:, cin, tt * 128:(tt + 1) * 128],
                    awv[:, cin, :], start=(cin == 0), stop=(cin == CH - 1)),
                    waits=[tws[2], xT_all, psB_tok], inc=(cin == CH - 1))
            vt_ev = sc.emit("dve", lambda v, tt=tt: v.tensor_copy(
                avtk[:, tt, :], h.psB[:, 0:128]), waits=[mm])
            psB_tok = vt_ev
        t_dz = sc.emit("pool", lambda g: g.memset(aDn[:, :], 0.0),
                       waits=[prev_pair_pe, dve0])
        dn_tok = t_dz
        wst_tok = tws

        for g in range(4 if B_STAGE >= 2 else 0):
            C_tok = [None, None, None, None]
            grp_ev = None
            for li in range(4):
                i = 4 * g + li
                trw = sc.emit("pe", lambda t, i=i: t.transpose(
                    h.psT[:, 1024:1152], awtok[:, i, :], h.eye_bf[:]),
                    waits=[wt_toks[i][0], wt_toks[i][1], pst1_tok])
                ev1 = sc.emit("dve", lambda v, li=li: v.tensor_copy(
                    awng[:, li, :], h.psT[:, 1024:1152]), waits=[trw])
                pst1_tok = ev1
                ev2 = sc.emit("dve", lambda v, li=li, i=i, pair=pair:
                              v.tensor_scalar_mul(
                    abwt[:, li, 0:64], awtok[:, i, 0:64],
                    h.beta[:, i, 2 * pair:2 * pair + 1]),
                    waits=[wt_toks[i][0], beta_tok])
                ev3 = sc.emit("dve", lambda v, li=li, i=i, pair=pair:
                              v.tensor_scalar_mul(
                    abwt[:, li, 64:128], awtok[:, i, 64:128],
                    h.beta[:, i, 2 * pair + 1:2 * pair + 2]),
                    waits=[wt_toks[i][1], beta_tok])
                trb = sc.emit("pe", lambda t, li=li: t.transpose(
                    h.psT[:, 1024:1152], abwt[:, li, :], h.eye_bf[:]),
                    waits=[ev2, ev3, pst1_tok])
                ev4 = sc.emit("dve", lambda v, li=li: v.tensor_copy(
                    abwn[:, li, :], h.psT[:, 1024:1152]), waits=[trb])
                pst1_tok = ev4
                grp_ev = ev4
            for h2 in range(2):
                rb = h2 * 64
                head = 2 * pair + h2
                t_fcr = sc.emit("sync", lambda y, head=head: y.dma_start(
                    out=fcrow[:, :], in_=FcT[head:head + 1, :]),
                    waits=[t_fc, fc_tok], dma="s0")
                for li in range(4 if B_STAGE >= 3 else 0):
                    i = 4 * g + li
                    jC = (i + 1) * 128
                    gate_pe = pe_now()
                    mm = sc.emit("pe", lambda t, li=li, rb=rb: t.matmul(
                        h.psC[:, 0:128], awng[rb:rb + 64, li, :],
                        awng[rb:rb + 64, li, :],
                        start=True, stop=True, tile_position=(rb, 0)),
                        waits=[grp_ev, psC_tok], inc=True)
                    tU = sc.emit("dve", lambda v, i=i, pair=pair, h2=h2:
                                 v.scalar_tensor_tensor(
                        nU, h.psC[:, 0:128],
                        h.beta[:, i, 2 * pair + h2:2 * pair + h2 + 1],
                        h.sumask[:], ALU.mult, ALU.mult),
                        waits=[mm, beta_tok, gate_pe])
                    psC_tok = tU
                    trv = sc.emit("pe", lambda t: t.transpose(
                        h.psT[:, 1152:1280], nU, h.eye_bf[:]), waits=[tU, pst1_tok])
                    tV = sc.emit("dve", lambda v: v.tensor_copy(
                        nV, h.psT[:, 1152:1280]), waits=[trv, gate_pe])
                    pst1_tok = tV
                    tAT = sc.emit("dve", lambda v: v.tensor_add(
                        nAT, nV, h.eye_bf[:]), waits=[tV, gate_pe])
                    tX = sc.emit("dve", lambda v: v.tensor_sub(
                        nXa, h.eye_bf[:], nU), waits=[tU, gate_pe])
                    tXT = sc.emit("dve", lambda v: v.tensor_sub(
                        nXTa, h.eye_bf[:], nV), waits=[tV, gate_pe])
                    Xc, XTc = nXa, nXTa
                    Xo, XTo = nXb, nXTb
                    for it in range(6):
                        mmy = sc.emit("pe", lambda t, Xc=Xc: t.matmul(
                            h.psC[:, 128:256], nAT, Xc, start=True, stop=True),
                            waits=[tAT, tX, psC_tok], inc=True)
                        tY = sc.emit("dve", lambda v: v.tensor_copy(
                            nY, h.psC[:, 128:256]), waits=[mmy, gate_pe])
                        psC_tok = tY
                        mmw = sc.emit("pe", lambda t, XTc=XTc: t.matmul(
                            h.psC[:, 128:256], XTc, nY, start=True, stop=True),
                            waits=[tY, tXT, psC_tok], inc=True)
                        tX = sc.emit("dve", lambda v, Xc=Xc, Xo=Xo: v.scalar_tensor_tensor(
                            Xo, Xc, 2.0, h.psC[:, 128:256], ALU.mult, ALU.subtract),
                            waits=[mmw, gate_pe])
                        psC_tok = tX
                        trx = sc.emit("pe", lambda t, Xo=Xo: t.transpose(
                            h.psT[:, 1152:1280], Xo, h.eye_bf[:]),
                            waits=[tX, pst1_tok])
                        tXT = sc.emit("dve", lambda v, XTo=XTo: v.tensor_copy(
                            XTo, h.psT[:, 1152:1280]), waits=[trx, gate_pe])
                        pst1_tok = tXT
                        Xc, XTc, Xo, XTo = Xo, XTo, Xc, XTc
                    tAf = sc.emit("dve", lambda v: v.tensor_copy(nATf, nV),
                                  waits=[tXT, tX, psC_tok])
                    tAf = sc.emit("dve", lambda v: v.tensor_add(
                        nATf, nATf, h.eye_f[:]), waits=[tAf])
                    tXf = sc.emit("dve", lambda v, Xc=Xc: v.tensor_copy(nXf, Xc),
                                  waits=[tX, tAf])
                    mmy = sc.emit("pe", lambda t: t.matmul(
                        h.psC[:, 128:256], nATf, nXf, start=True, stop=True),
                        waits=[tXf, psC_tok], inc=True)
                    tYf = sc.emit("dve", lambda v: v.tensor_copy(
                        nYf, h.psC[:, 128:256]), waits=[mmy, gate_pe])
                    psC_tok = tYf
                    tXTf = sc.emit("dve", lambda v, XTc=XTc: v.tensor_copy(nWf, XTc),
                                   waits=[tYf])
                    mmw = sc.emit("pe", lambda t: t.matmul(
                        h.psC[:, 256:384], nYf, nWf, start=True, stop=True),
                        waits=[tXTf, psC_tok], inc=True)
                    t_inv = sc.emit("dve", lambda v, li=li: v.scalar_tensor_tensor(
                        ainv[:, li, :], nWf, 2.0, h.psC[:, 256:384],
                        ALU.mult, ALU.subtract), waits=[mmw, gate_pe])
                    psC_tok = t_inv
                    if B_STAGE < 4:
                        continue
                    mm = sc.emit("pe", lambda t, i=i, li=li, rb=rb: t.matmul(
                        h.psC[rb:rb + 64, 0:128], awtok[:, i, rb:rb + 64],
                        ainv[:, li, :], start=True, stop=True, tile_position=(0, rb)),
                        waits=[t_inv, psC_tok], inc=True)
                    tWT = sc.emit("dve", lambda v, rb=rb: v.tensor_copy(
                        aWT[rb:rb + 64, :], h.psC[rb:rb + 64, 0:128]),
                        waits=[mm, pe_now()])
                    psC_tok = tWT
                    lhat_tok = {}
                    for lc in range(li):
                        mm1 = sc.emit("pe", lambda t, li=li, lc=lc, rb=rb: t.matmul(
                            h.psC[:, 128:256], awng[rb:rb + 64, li, :],
                            abwn[rb:rb + 64, lc, :], start=True, stop=True,
                            tile_position=(rb, 0)), waits=[psC_tok], inc=True)
                        tL = sc.emit("dve", lambda v: v.tensor_copy(
                            aLic, h.psC[:, 128:256]), waits=[mm1, pe_now()])
                        mm2 = sc.emit("pe", lambda t, li=li: t.matmul(
                            h.psC[:, 256:384], aLic, ainv[:, li, :],
                            start=True, stop=True), waits=[tL], inc=True)
                        tLh = sc.emit("dve", lambda v, lc=lc: v.tensor_scalar_mul(
                            aLhat[lc], h.psC[:, 256:384], -1.0),
                            waits=[mm2, pe_now()])
                        psC_tok = tLh
                        lhat_tok[lc] = tLh
                    mm3 = sc.emit("pe", lambda t, i=i, li=li, rb=rb: t.matmul(
                        h.psC[:, 128:256], awng[rb:rb + 64, li, :],
                        akT[rb:rb + 64, i * 128:(i + 1) * 128], start=True, stop=True,
                        tile_position=(rb, 0)), waits=[tkv, psC_tok], inc=True)
                    tSL = sc.emit("dve", lambda v: v.scalar_tensor_tensor(
                        aSLWK, h.psC[:, 128:256], 1.0, h.slmask[:],
                        ALU.mult, ALU.mult), waits=[mm3, pe_now()])
                    psC_tok = tSL
                    c_regs = []
                    for j0 in range(0, jC, 1024):
                        jn = min(1024, jC - j0)
                        r = rega % 2
                        rega += 1
                        ps = h.psA[:, r * 1024:r * 1024 + jn]
                        mm = None
                        for ks in range(0, jn, 512):
                            b0 = j0 + ks
                            kn = min(512, jn - ks)
                            emitted = []
                            emitted.append((ks, kn, aWT[rb:rb + 64, :],
                                            aDn[rb:rb + 64, b0:b0 + kn], (rb, 0),
                                            [tWT, dn_tok]))
                            kw = min(kn, max(0, i * 128 - b0))
                            if kw > 0:
                                emitted.append((ks, kw, aWT[rb:rb + 64, :],
                                                akT[rb:rb + 64, b0:b0 + kw], (rb, 0),
                                                [tWT, tkv]))
                            for lc in range(li):
                                c = 4 * g + lc
                                lw = min(kn, max(0, (c + 1) * 128 - b0))
                                if lw > 0:
                                    emitted.append((ks, lw, aLhat[lc],
                                                    aC[:, lc, b0:b0 + lw], None,
                                                    [lhat_tok[lc], C_tok[lc]]))
                            ia = max(i * 128, b0)
                            ib = min(jC, b0 + kn)
                            if ib > ia:
                                emitted.append((ia - j0, ib - ia, ainv[:, li, :],
                                                aSLWK[:, ia - i * 128:ib - i * 128],
                                                None, [t_inv, tSL]))
                            for ti, (co, cw, lhsT, rhs, tile, tw) in enumerate(emitted):
                                mm = sc.emit("pe", lambda t, ps=ps, co=co, cw=cw,
                                             lhsT=lhsT, rhs=rhs, tile=tile,
                                             first=(ti == 0),
                                             last=(ti == len(emitted) - 1): t.matmul(
                                    ps[:, co:co + cw], lhsT, rhs, start=first,
                                    stop=last, tile_position=tile),
                                    waits=tw + [regA_free[r]],
                                    inc=(ti == len(emitted) - 1))
                        cev = sc.emit("dve", lambda v, ps=ps, li=li, j0=j0, jn=jn:
                                      v.tensor_copy(aC[:, li, j0:j0 + jn], ps[:]),
                                      waits=[mm, pe_now()])
                        regA_free[r] = cev
                        C_tok[li] = cev
                    if B_STAGE < 5:
                        continue
                    ght_tok = {}
                    for lc in range(li + 1):
                        mmg = sc.emit("pe", lambda t, li=li, lc=lc, rb=rb, i=i:
                                      t.matmul(
                            h.psC[:, 128:256], abwn[rb:rb + 64, lc, :],
                            aqT[rb:rb + 64, i * 128:(i + 1) * 128],
                            start=True, stop=True, tile_position=(rb, 0)),
                            waits=[tqv, grp_ev, psC_tok], inc=True)
                        if lc == li:
                            tG = sc.emit("dve", lambda v, lc=lc: v.scalar_tensor_tensor(
                                aGht[lc], h.psC[:, 128:256], -1.0, h.utmask[:],
                                ALU.mult, ALU.mult), waits=[mmg, pe_now()])
                        else:
                            tG = sc.emit("dve", lambda v, lc=lc: v.tensor_scalar_mul(
                                aGht[lc], h.psC[:, 128:256], -1.0),
                                waits=[mmg, pe_now()])
                        psC_tok = tG
                        ght_tok[lc] = tG
                    lg_regs = []
                    for j0 in range(0, jC, 1024):
                        jn = min(1024, jC - j0)
                        r = rega % 2
                        rega += 1
                        ps = h.psA[:, r * 1024:r * 1024 + jn]
                        mm = None
                        for ks in range(0, jn, 512):
                            b0 = j0 + ks
                            kn = min(512, jn - ks)
                            emitted = []
                            emitted.append((ks, kn, aqT[rb:rb + 64,
                                                        i * 128:(i + 1) * 128],
                                            akT[rb:rb + 64, b0:b0 + kn], (rb, 0),
                                            [tqv, tkv]))
                            emitted.append((ks, kn, aqT[rb:rb + 64,
                                                        i * 128:(i + 1) * 128],
                                            aDn[rb:rb + 64, b0:b0 + kn], (rb, 0),
                                            [tqv, dn_tok]))
                            for lc in range(li + 1):
                                c = 4 * g + lc
                                lw = min(kn, max(0, (c + 1) * 128 - b0))
                                if lw > 0:
                                    emitted.append((ks, lw, aGht[lc],
                                                    aC[:, lc, b0:b0 + lw], None,
                                                    [ght_tok[lc], C_tok[lc]]))
                            for ti, (co, cw, lhsT, rhs, tile, tw) in enumerate(emitted):
                                sc.emit("pe", lambda t, ps=ps, co=co, cw=cw,
                                        lhsT=lhsT, rhs=rhs, tile=tile,
                                        first=(ti == 0): t.matmul(
                                    ps[:, co:co + cw], lhsT, rhs, start=first,
                                    stop=False, tile_position=tile),
                                    waits=tw + [regA_free[r]], inc=False)
                            ia = max(i * 128, b0)
                            ib = min(jC, b0 + kn)
                            if ib > ia:
                                sc.emit("pe", lambda t, ps=ps, ia=ia, ib=ib,
                                        j0=j0: t.matmul(
                                    ps[:, ia - j0:ib - j0], h.eye_bf[:], h.mneg[:],
                                    start=False, stop=False),
                                    waits=[h.const_tok], inc=False)
                            mm = sc.emit("pe", lambda t, ps=ps, ks=ks, kn=kn, b0=b0:
                                         t.matmul(
                                ps[:, ks:ks + kn], h.onesr[:],
                                fcrow[0:1, b0:b0 + kn], start=False, stop=True),
                                waits=[t_fcr], inc=True)
                        lg_regs.append((r, ps, j0, jn, mm))
                    sl_ids = []
                    sub_toks = []
                    for (r, ps, j0, jn, mm) in lg_regs:
                        sl_i = small_ctr % 12
                        small_ctr += 1
                        sl_ids.append(sl_i)
                        nm_ap = h.small[:, sl_i * 4: sl_i * 4 + 1]
                        z_ap = h.small[:, sl_i * 4 + 1: sl_i * 4 + 2]
                        t_nm = sc.emit("dve", lambda v, ps=ps, nm_ap=nm_ap:
                                       v.reduce_max(
                            nm_ap, ps[:], axis=AX.X, negate=True),
                            waits=[mm] + slot_free.get(sl_i, []))
                        t_z = sc.emit("act", lambda a, ps=ps, nm_ap=nm_ap, z_ap=z_ap,
                                      jn=jn: a.activation(
                            h.dump[:, 1024:1024 + jn], ps[:], AF.Exp, bias=nm_ap,
                            scale=1.0, accum_out=z_ap), waits=[t_nm])
                        sub_toks.append((t_nm, t_z, nm_ap, z_ap))
                    sl0 = sl_ids[0]
                    nm0 = h.small[:, sl0 * 4: sl0 * 4 + 1]
                    z0 = h.small[:, sl0 * 4 + 1: sl0 * 4 + 2]
                    lz = h.small[:, sl0 * 4 + 2: sl0 * 4 + 3]
                    b2 = h.small[:, sl0 * 4 + 3: sl0 * 4 + 4]
                    if len(lg_regs) == 1:
                        t_lz = sc.emit("act", lambda a, z0=z0, lz=lz: a.activation(
                            lz, z0, AF.Ln, bias=0.0, scale=1.0),
                            waits=[sub_toks[0][1]])
                        t_b2 = sc.emit("dve", lambda v, nm0=nm0, lz=lz, b2=b2:
                                       v.tensor_sub(b2, nm0, lz), waits=[t_lz])
                    else:
                        sl1 = sl_ids[1]
                        nm1 = h.small[:, sl1 * 4: sl1 * 4 + 1]
                        z1 = h.small[:, sl1 * 4 + 1: sl1 * 4 + 2]
                        d0 = h.small[:, sl1 * 4 + 2: sl1 * 4 + 3]
                        d1 = h.small[:, sl1 * 4 + 3: sl1 * 4 + 4]
                        tmin = sc.emit("dve", lambda v, nm0=nm0, nm1=nm1, d0=d0:
                                       v.tensor_tensor(d0, nm0, nm1, ALU.min),
                                       waits=[sub_toks[0][0], sub_toks[1][0]])
                        t1 = sc.emit("dve", lambda v, d0=d0, nm0=nm0, d1=d1:
                                     v.tensor_sub(d1, d0, nm0), waits=[tmin])
                        te1 = sc.emit("act", lambda a, d1=d1: a.activation(
                            d1, d1, AF.Exp, bias=0.0, scale=1.0), waits=[t1])
                        t2 = sc.emit("dve", lambda v, z0=z0, d1=d1: v.tensor_mul(
                            z0, z0, d1), waits=[te1, sub_toks[0][1]])
                        t3 = sc.emit("dve", lambda v, d1=d1, d0=d0, nm1=nm1:
                                     v.tensor_sub(d1, d0, nm1), waits=[t2])
                        te2 = sc.emit("act", lambda a, d1=d1: a.activation(
                            d1, d1, AF.Exp, bias=0.0, scale=1.0), waits=[t3])
                        t4 = sc.emit("dve", lambda v, z1=z1, d1=d1: v.tensor_mul(
                            z1, z1, d1), waits=[te2, sub_toks[1][1]])
                        t5 = sc.emit("dve", lambda v, z0=z0, z1=z1: v.tensor_add(
                            z0, z0, z1), waits=[t4])
                        tnm = sc.emit("dve", lambda v, nm0=nm0, d0=d0: v.tensor_copy(
                            nm0, d0), waits=[tmin])
                        t_lz = sc.emit("act", lambda a, z0=z0, lz=lz: a.activation(
                            lz, z0, AF.Ln, bias=0.0, scale=1.0), waits=[t5])
                        t_b2 = sc.emit("dve", lambda v, nm0=nm0, lz=lz, b2=b2:
                                       v.tensor_sub(b2, nm0, lz), waits=[t_lz, tnm])
                        slot_free[sl1] = [t_b2]
                    slot_free[sl0] = [t_b2]
                    for si, (r, ps, j0, jn, mm) in enumerate(lg_regs):
                        t_sp = sc.emit("dve", lambda v, ps=ps, jn=jn, b2=b2:
                                       v.tensor_scalar_add(aspp[:, 0:jn], ps[:], b2),
                                       waits=[t_b2, spp_tok])
                        regA_free[r] = t_sp
                        slot_free[sl_ids[si]] = slot_free.get(sl_ids[si], []) + [t_sp]
                        nkt = jn // 128
                        pst = h.psT[:, 0:jn]
                        tr = None
                        for kt in range(nkt):
                            tr = sc.emit("pe", lambda t, pst=pst, kt=kt: t.transpose(
                                pst[:, kt * 128:(kt + 1) * 128],
                                aspp[:, kt * 128:(kt + 1) * 128],
                                h.eye_bf[:]), waits=[t_sp, pst0_tok],
                                inc=(kt == nkt - 1))
                        spp_tok = tr
                        t_pt = sc.emit("act", lambda a, pst=pst, j0=j0, nkt=nkt:
                                       a.activation(
                            aPT[:, j0 // 128: j0 // 128 + nkt, :],
                            pst.rearrange("p (k q) -> p k q", q=128), AF.Exp,
                            bias=0.0, scale=1.0), waits=[tr, pt_tok])
                        pst0_tok = t_pt
                    pv = None
                    for kt in range(i + 1):
                        pv = sc.emit("pe", lambda t, kt=kt, rb=rb, li=li, i=i:
                                     t.matmul(
                            h.psB[rb:rb + 64, li * 128:(li + 1) * 128],
                            avtk[:, kt, rb:rb + 64], aPT[:, kt, :],
                            start=(kt == 0), stop=(kt == i),
                            tile_position=(0, rb)),
                            waits=[pst0_tok, vt_ev, psB_tok], inc=(kt == i))
                    pt_tok = pv
                    if h2 == 1:
                        ev = sc.emit("dve", lambda v, li=li, i=i, pair=pair:
                                     v.tensor_copy(
                            aOT[:, pair, i * 128:(i + 1) * 128],
                            h.psB[:, li * 128:(li + 1) * 128]), waits=[pv])
                        psB_tok = ev
                fc_tok = pe_now()
                jg = (4 * g + 4) * 128
                for j0 in (range(0, jg, 1024) if B_STAGE >= 6 else []):
                    jn = min(1024, jg - j0)
                    r = rega % 2
                    rega += 1
                    ps = h.psA[:, r * 1024:r * 1024 + jn]
                    mm = None
                    for ks in range(0, jn, 512):
                        b0 = j0 + ks
                        kn = min(512, jn - ks)
                        emitted = []
                        for lc in (3, 2, 1, 0):
                            c = 4 * g + lc
                            lw = min(kn, max(0, (c + 1) * 128 - b0))
                            if lw > 0:
                                emitted.append((ks, lw, lc))
                        for ti, (co, cw, lc) in enumerate(emitted):
                            mm = sc.emit("pe", lambda t, ps=ps, co=co, cw=cw, lc=lc,
                                         rb=rb, b0=b0, first=(ti == 0),
                                         last=(ti == len(emitted) - 1): t.matmul(
                                ps[rb:rb + 64, co:co + cw], abwt[:, lc, rb:rb + 64],
                                aC[:, lc, b0:b0 + cw], start=first, stop=last,
                                tile_position=(0, rb)),
                                waits=[grp_ev, C_tok[lc], regA_free[r]],
                                inc=(ti == len(emitted) - 1))
                    if B_STAGE < 6.5:
                        mmdis = sc.emit("dve", lambda v, ps=ps, rb=rb, jn=jn:
                                        v.tensor_copy(
                            h.dump[:, 1024:1024 + jn][rb:rb + 64, :],
                            ps[rb:rb + 64, :]), waits=[mm, pe_now()])
                        regA_free[r] = mmdis
                        continue
                    dcp = sc.emit("act", lambda a, ps=ps, rb=rb, jn=jn:
                                  a.activation(
                        h.dump[:, 1024:1024 + jn][rb:rb + 64, :], ps[rb:rb + 64, :],
                        AF.Identity, bias=0.0, scale=-1.0), waits=[mm, pe_now()])
                    dup = sc.emit("dve", lambda v, rb=rb, j0=j0, jn=jn:
                                  v.tensor_add(
                        aDn[rb:rb + 64, j0:j0 + jn], aDn[rb:rb + 64, j0:j0 + jn],
                        h.dump[:, 1024:1024 + jn][rb:rb + 64, :]),
                        waits=[dcp])
                    regA_free[r] = dup
                    dn_tok = dup
        prev_pair_pe = pe_now()

    t_wo = None
    for cin in (range(CH) if B_STAGE >= 7 else []):
        t_wo = sc.emit("pool", lambda g, cin=cin: g.dma_start(
            out=awo_final[:, cin, :], in_=d["bwo"][cin * 128:(cin + 1) * 128, :]),
            waits=[prev_pair_pe, dve0], dma="g2")
    psBC = [h.psB, h.psC]
    bctok = [psB_tok, psC_tok]
    for tt in (range(TT) if B_STAGE >= 7 else []):
        for hf in range(2):
            u = hf
            ps = psBC[u]
            mm = None
            for cin in range(CH):
                mm = sc.emit("pe", lambda t, ps=ps, tt=tt, cin=cin, hf=hf: t.matmul(
                    ps[:, 0:512], aOT[:, cin, tt * 128:(tt + 1) * 128],
                    awo_final[:, cin, hf * 512:hf * 512 + 512],
                    start=(cin == 0), stop=(cin == CH - 1)),
                    waits=[t_wo, psB_tok, psC_tok, bctok[u]], inc=(cin == CH - 1))
            t_s = sc.emit("dve", lambda v, ps=ps, tt=tt, hf=hf: v.tensor_add(
                h.s[:, tt, hf * 512:hf * 512 + 512],
                h.s[:, tt, hf * 512:hf * 512 + 512], ps[:, 0:512]),
                waits=[mm, s2_tok[tt]])
            bctok[u] = t_s
            s2_tok[tt] = t_s

    for tt in range(TT):
        s_tok[tt] = s2_tok[tt]
    return {"pe": ("pe", sc.cnt["pe"]), "dve": ("dve", sc.cnt["dve"]),
            "act": ("act", sc.cnt["act"])}



def build_nc(n_sub=4):
    nc = bass.Bass("TRN2", target_bir_lowering=False)
    h = H()

    dx = nc.dram_tensor("x", [T, C], F32, kind="ExternalInput").ap()
    d_wqkv = nc.dram_tensor("wqkv", [C, 3 * C], F32, kind="ExternalInput").ap()
    d_bqkv = nc.dram_tensor("bqkv", [128, 24], F32, kind="ExternalInput").ap()
    d_wouta = nc.dram_tensor("wouta", [C + 1, C], F32, kind="ExternalInput").ap()
    d_fwq = nc.dram_tensor("fwq", [C, C], F32, kind="ExternalInput").ap()
    d_fwk = nc.dram_tensor("fwk", [C, 256], F32, kind="ExternalInput").ap()
    d_fwv = nc.dram_tensor("fwv", [C, 256], F32, kind="ExternalInput").ap()
    d_fwo = nc.dram_tensor("fwo", [C, C], F32, kind="ExternalInput").ap()
    d_fgate = nc.dram_tensor("fgate", [C, 8], F32, kind="ExternalInput").ap()
    d_fexp = nc.dram_tensor("fexp", [8, C, C], F32, kind="ExternalInput").ap()
    d_fnw = nc.dram_tensor("fnw", [128, 16], F32, kind="ExternalInput").ap()
    d_bwq = nc.dram_tensor("bwq", [C, C], F32, kind="ExternalInput").ap()
    d_bwk = nc.dram_tensor("bwk", [C, C], F32, kind="ExternalInput").ap()
    d_bwv = nc.dram_tensor("bwv", [C, C], F32, kind="ExternalInput").ap()
    d_bwo = nc.dram_tensor("bwo", [C, C], F32, kind="ExternalInput").ap()
    d_bww1 = nc.dram_tensor("bww1", [C, 64], F32, kind="ExternalInput").ap()
    d_bww2 = nc.dram_tensor("bww2", [64, C], F32, kind="ExternalInput").ap()
    d_bwbf = nc.dram_tensor("bwbf", [C, 32], F32, kind="ExternalInput").ap()
    d_bdelta = nc.dram_tensor("bdelta", [1, 16], F32, kind="ExternalInput").ap()
    d_out = nc.dram_tensor("out", [T, C], F32, kind="ExternalOutput").ap()

    es = ExitStack()
    with es:
        def sb(name, shape, dt=F32):
            return es.enter_context(nc.sbuf_tensor(name, shape, dt))

        h.s = sb("s", [128, TT, C])
        h.xT = sb("xT", [128, CH, T], BF16)
        h.dump = sb("dump", [128, T], BF16)
        h.f32buf = sb("f32buf", [128, T])
        h.arena = sb("arena", [128, 44928], BF16)
        h.ybf = sb("ybf", [128, 2, C], BF16)
        for nm in ("nsum", "sumsq", "nm", "var", "sd", "rr", "nmr", "tmp16",
                   "rr2", "nmr2", "msq2", "rq2", "w16a", "w16b"):
            setattr(h, nm, sb(nm, [128, 16]))
        h.comb = sb("comb", [128, 16, 8])
        h.mneg = sb("mneg", [128, 128], BF16)
        h.eye_f = sb("eye_f", [128, 128])
        h.sumask = sb("sumask", [128, 128], BF16)
        h.slmask = sb("slmask", [128, 128], BF16)
        h.utmask = sb("utmask", [128, 128], BF16)
        h.blf = sb("blf", [128, 512])
        h.beta = sb("beta", [128, 16, 16])
        h.small = sb("small", [128, 96])
        h.bq_sb = sb("bq_sb", [128, 24])
        h.eye_bf = sb("eye_bf", [128, 128], BF16)
        h.onesr = sb("onesr", [1, 128], F32)
        h.ones_bf = sb("ones_bf", [1, 128], BF16)
        h.fnw_sb = sb("fnw_sb", [128, 2])
        h.fnw16 = sb("fnw16", [128, 16])
        h.psA = es.enter_context(nc.psum_tensor("psA", [128, 2048], F32))
        h.psT = es.enter_context(nc.psum_tensor("psT", [128, 2048], BF16))
        h.psB = es.enter_context(nc.psum_tensor("psB", [128, 512], F32))
        h.psC = es.enter_context(nc.psum_tensor("psC", [128, 512], F32))

        sems = {}
        for s in ("pe", "dve", "act", "pool", "sync", "g0", "g1", "g2", "g3", "s0"):
            sems[s] = es.enter_context(nc.semaphore(s))
        blk = es.enter_context(nc.Block())

        sc = Sched()

        t_c = sc.emit("pool", lambda g: g.memset(h.eye_bf[:], 1.0))
        t_c = sc.emit("pool", lambda g: g.affine_select(
            h.eye_bf[:], h.eye_bf[:], [[1, 128]], ALU.is_equal, 0.0,
            base=0, channel_multiplier=-1), waits=[t_c])
        t_c = sc.emit("pool", lambda g: g.memset(h.onesr[:], 1.0), waits=[t_c])
        t_c = sc.emit("pool", lambda g: g.memset(h.ones_bf[:], 1.0), waits=[t_c])
        t_c = sc.emit("pool", lambda g: g.memset(h.mneg[:], 0.0), waits=[t_c])
        t_c = sc.emit("pool", lambda g: g.affine_select(
            h.mneg[:], h.mneg[:], [[-1, 128]], ALU.is_ge, -200.0,
            base=0, channel_multiplier=1), waits=[t_c])
        t_c = sc.emit("pool", lambda g: g.memset(h.eye_f[:], 1.0), waits=[t_c])
        t_c = sc.emit("pool", lambda g: g.affine_select(
            h.eye_f[:], h.eye_f[:], [[1, 128]], ALU.is_equal, 0.0,
            base=0, channel_multiplier=-1), waits=[t_c])
        t_c = sc.emit("pool", lambda g: g.memset(h.sumask[:], 1.0), waits=[t_c])
        t_c = sc.emit("pool", lambda g: g.affine_select(
            h.sumask[:], h.sumask[:], [[1, 128]], ALU.is_ge, 0.0,
            base=-1, channel_multiplier=-1), waits=[t_c])
        t_c = sc.emit("pool", lambda g: g.memset(h.slmask[:], 1.0), waits=[t_c])
        t_c = sc.emit("pool", lambda g: g.affine_select(
            h.slmask[:], h.slmask[:], [[-1, 128]], ALU.is_ge, 0.0,
            base=-1, channel_multiplier=1), waits=[t_c])
        t_c = sc.emit("pool", lambda g: g.memset(h.utmask[:], 1.0), waits=[t_c])
        t_c = sc.emit("pool", lambda g: g.affine_select(
            h.utmask[:], h.utmask[:], [[1, 128]], ALU.is_ge, 0.0,
            base=0, channel_multiplier=-1), waits=[t_c])
        h.const_tok = t_c
        t_eye = t_c
        t_ones = t_c
        t_mneg = t_c

        tok_s_in = sc.emit("sync", lambda y: y.dma_start(
            out=h.s[:], in_=dx.rearrange("(tt p) c -> p tt c", p=128)), dma="s0")

        t_bq = sc.emit("pool", lambda g: g.dma_start(
            out=h.bq_sb[:], in_=d_bqkv), dma="g2")
        h.fnw_tok = sc.emit("pool", lambda g: g.dma_start(
            out=h.fnw16[:], in_=d_fnw), waits=[t_bq], dma="g2")
        t_bqs = sc.emit("pool", lambda g: g.tensor_scalar_mul(
            h.bq_sb[:, 0:8], h.bq_sb[:, 0:8], 0.125), waits=[t_bq])

        W = 512
        NW = 4
        TA = 1280
        TAT = 10

        off = 0
        def carve(n):
            nonlocal off
            ap = h.arena[:, off:off + n]
            off += n
            return ap
        aQT = carve(8 * TA).rearrange("p (c t) -> p c t", t=TA)
        aKT = carve(8 * TA).rearrange("p (c t) -> p c t", t=TA)
        aV = carve(8 * TAT * 128).rearrange("p (c t f) -> p c t f", t=TAT, f=128)
        aOT = carve(8 * 512).rearrange("p (c t) -> p c t", t=512)
        awu = carve(8 * 1024)
        awb = [awu[:, i * 4096:(i + 1) * 4096].rearrange("p (c t) -> p c t", t=512)
               for i in range(2)]
        aWO = awu.rearrange("p (c t) -> p c t", t=1024)
        aVS = h.dump[:, 0:TA]
        aSpp = [h.dump[:, 0:512], h.dump[:, 512:1024]]
        abrow = carve(1024)
        aPT = h.ybf[:].rearrange("p a f -> p (a f)").rearrange(
            "p (k q) -> p k q", q=512)
        assert off <= 44928, off

        t_rr, t_nmr = _ln_stats(sc, h, list(range(TAT)), tok_s_in, 0)
        tok_xT, psT_free = _norm_apply_transpose(sc, h, list(range(TAT)), t_rr, t_nmr, tok_s_in)

        psA_free = [None, None, None, None]
        psT_half_free = psT_free
        wtok = [None, None]
        wfree = [None, None]
        slab_sizes = [(0, 512), (512, 512), (1024, 256)]
        slot_ctr = 0
        vstage_tok = {}
        tok_QT = {}
        tok_KT = {}
        tok_V = {}
        vs_free = None
        for part in range(3):
            for hf in range(2):
                bi = (part * 2 + hf) % 2
                last = None
                for cin in range(CH):
                    last = sc.emit("pool", lambda g, part=part, hf=hf, cin=cin, bi=bi: g.dma_start(
                        out=awb[bi][:, cin, :],
                        in_=d_wqkv[cin * 128:(cin + 1) * 128,
                                   part * 1024 + hf * 512: part * 1024 + hf * 512 + 512]),
                        waits=[wfree[bi]], dma="g0" if bi == 0 else "g1")
                wtok[bi] = last
                for pr in range(4):
                    cc = part * 8 + hf * 4 + pr
                    pair = hf * 4 + pr
                    ev_last = None
                    for (t0, n) in slab_sizes:
                        slot = slot_ctr % 4
                        slot_ctr += 1
                        ps = h.psA[:, slot * 512: slot * 512 + n]
                        mm = None
                        for cin in range(CH):
                            need_t = tok_xT[min(t0 // 128 + (n - 1) // 128, TAT - 1)]
                            mm = sc.emit("pe", lambda t, ps=ps, bi=bi, cin=cin, pr=pr, t0=t0, n=n: t.matmul(
                                ps, awb[bi][:, cin, pr * 128:(pr + 1) * 128],
                                h.xT[:, cin, t0:t0 + n],
                                start=(cin == 0), stop=(cin == CH - 1)),
                                waits=[wtok[bi], need_t, psA_free[slot]],
                                inc=(cin == CH - 1))
                        if part == 0:
                            ev = sc.emit("act", lambda a, ps=ps, pair=pair, t0=t0, n=n, cc=cc: a.activation(
                                aQT[:, pair, t0:t0 + n], ps, AF.Identity,
                                bias=h.bq_sb[:, cc:cc + 1], scale=0.125),
                                waits=[mm, t_bqs])
                            tok_QT[pair] = ev
                        elif part == 1:
                            ev = sc.emit("act", lambda a, ps=ps, pair=pair, t0=t0, n=n, cc=cc: a.activation(
                                aKT[:, pair, t0:t0 + n], ps, AF.Identity,
                                bias=h.bq_sb[:, cc:cc + 1], scale=1.0),
                                waits=[mm, t_bq])
                            tok_KT[pair] = ev
                        else:
                            ev = sc.emit("act", lambda a, ps=ps, t0=t0, n=n, cc=cc: a.activation(
                                aVS[:, t0:t0 + n], ps, AF.Identity,
                                bias=h.bq_sb[:, cc:cc + 1], scale=1.0),
                                waits=[mm, t_bq, vs_free])
                        psA_free[slot] = ev
                        ev_last = ev
                    if part == 2:
                        tr_tok = None
                        for tt in range(TAT):
                            halfp = tt % 2
                            pst = h.psT[:, halfp * 1024 + 512: halfp * 1024 + 1024]
                            tr = sc.emit("pe", lambda t, pst=pst, tt=tt: t.transpose(
                                pst[:, 0:128], aVS[:, tt * 128:(tt + 1) * 128], h.eye_bf[:]),
                                waits=[ev_last, psT_half_free[halfp]])
                            ev2 = sc.emit("dve", lambda v, pst=pst, pair=pair, tt=tt: v.tensor_copy(
                                aV[:, pair, tt, :], pst[:, 0:128]), waits=[tr])
                            psT_half_free[halfp] = ev2
                            tr_tok = ev2
                        tok_V[pair] = tr_tok
                        vs_free = tr_tok
                wfree[bi] = ev_last if part < 2 else vs_free

        t_wo = None
        for cin in range(CH):
            t_wo = sc.emit("pool", lambda g, cin=cin: g.dma_start(
                out=aWO[:, cin, :], in_=d_wouta[cin * 128:(cin + 1) * 128, :]),
                waits=[wfree[0], wfree[1], h.fnw_tok], dma="g2")
        t_brow = sc.emit("pool", lambda g: g.dma_start(
            out=abrow[0:1, :], in_=d_wouta[C:C + 1, :]), waits=[t_wo], dma="g2")

        s_tok = {}
        dump_guard = vs_free
        small_ctr = 0
        slot_free = {}
        spp_tok = [None, None]
        pt_tok = None
        psA_free2 = psA_free
        psBC = [h.psB, h.psC]
        psBC_free = [None, None]
        unit = 0
        tok_oT = {}
        wout_done = None
        for win in range(NW):
            tok_oT[win] = {}
            for pair in range(8):
                o_slot = None
                for h2 in range(2):
                    rb = h2 * 64
                    pv_deps = []
                    for qt in range(4):
                        u = unit % 2
                        unit += 1
                        ps = psBC[u]
                        q0 = win * 256 + qt * 128
                        mm = sc.emit("pe", lambda t, ps=ps, pair=pair, rb=rb, q0=q0, win=win: t.matmul(
                            ps[:], aQT[rb:rb + 64, pair, q0:q0 + 128],
                            aKT[rb:rb + 64, pair, win * 256:win * 256 + 512],
                            start=True, stop=True, tile_position=(rb, 0)),
                            waits=[tok_QT[pair], tok_KT[pair], psBC_free[u]])
                        sl = small_ctr % 16
                        small_ctr += 1
                        nm_ap = h.small[:, sl * 4 + 0: sl * 4 + 1]
                        z_ap = h.small[:, sl * 4 + 1: sl * 4 + 2]
                        lz_ap = h.small[:, sl * 4 + 2: sl * 4 + 3]
                        b2_ap = h.small[:, sl * 4 + 3: sl * 4 + 4]
                        t_nm = sc.emit("dve", lambda v, ps=ps, nm_ap=nm_ap: v.reduce_max(
                            nm_ap, ps[:], axis=AX.X, negate=True),
                            waits=[mm] + slot_free.get(sl, []))
                        t_z = sc.emit("act", lambda a, ps=ps, nm_ap=nm_ap, z_ap=z_ap: a.activation(
                            h.dump[:, 1024:1536], ps[:], AF.Exp, bias=nm_ap, scale=1.0,
                            accum_out=z_ap), waits=[t_nm, mm, dump_guard])
                        t_lz = sc.emit("act", lambda a, z_ap=z_ap, lz_ap=lz_ap: a.activation(
                            lz_ap, z_ap, AF.Ln, bias=0.0, scale=1.0), waits=[t_z])
                        t_b2 = sc.emit("dve", lambda v, nm_ap=nm_ap, lz_ap=lz_ap, b2_ap=b2_ap: v.tensor_sub(
                            b2_ap, nm_ap, lz_ap), waits=[t_lz])
                        spp = aSpp[u]
                        t_sp = sc.emit("dve", lambda v, ps=ps, b2_ap=b2_ap, spp=spp: v.tensor_scalar_add(
                            spp, ps[:], b2_ap), waits=[t_b2, spp_tok[u]])
                        psBC_free[u] = t_sp
                        slot_free[sl] = [t_sp, t_lz]
                        tq = unit % 2
                        pst = h.psT[:, tq * 1024: tq * 1024 + 512]
                        tr = None
                        for kt in range(4):
                            tr = sc.emit("pe", lambda t, pst=pst, spp=spp, kt=kt: t.transpose(
                                pst[:, kt * 128:(kt + 1) * 128], spp[:, kt * 128:(kt + 1) * 128],
                                h.eye_bf[:]), waits=[t_sp, psT_half_free[tq]],
                                inc=(kt == 3))
                        spp_tok[u] = tr
                        t_pt = sc.emit("act", lambda a, pst=pst, qt=qt: a.activation(
                            aPT[:, :, qt * 128:(qt + 1) * 128],
                            pst, AF.Exp, bias=0.0, scale=1.0),
                            waits=[tr, pt_tok])
                        psT_half_free[tq] = t_pt
                        pv_deps.append(t_pt)
                    if o_slot is None:
                        o_slot = (win * 8 + pair) % 4
                    pso = h.psA[:, o_slot * 512: o_slot * 512 + 512]
                    pv = None
                    for kt in range(4):
                        pv = sc.emit("pe", lambda t, pso=pso, pair=pair, win=win, kt=kt, rb=rb: t.matmul(
                            pso[rb:rb + 64, :], aV[:, pair, win * 2 + kt, rb:rb + 64],
                            aPT[:, kt, :], start=(kt == 0), stop=(kt == 3),
                            tile_position=(0, rb)),
                            waits=pv_deps + [tok_V[pair], psA_free2[o_slot]],
                            inc=(kt == 3))
                    pt_tok = pv
                    if h2 == 1:
                        ev = sc.emit("dve", lambda v, pso=pso, pair=pair: v.tensor_copy(
                            aOT[:, pair, :], pso[:]), waits=[pv, wout_done])
                        psA_free2[o_slot] = ev
                        tok_oT[win][pair] = ev
            for tl in range(4):
                tt = win * 4 + tl
                for hf in range(2):
                    u = (tt * 2 + hf) % 2
                    ps = psBC[u]
                    mm = None
                    for cin in range(CH):
                        mm = sc.emit("pe", lambda t, ps=ps, cin=cin, tl=tl, hf=hf, win=win: t.matmul(
                            ps[:], aOT[:, cin, tl * 128:(tl + 1) * 128],
                            aWO[:, cin, hf * 512:hf * 512 + 512],
                            start=(cin == 0), stop=False),
                            waits=[t_wo, tok_oT[win][cin], psBC_free[u]], inc=False)
                    mm = sc.emit("pe", lambda t, ps=ps, hf=hf: t.matmul(
                        ps[:], h.ones_bf[0:1, :],
                        abrow[0:1, hf * 512:hf * 512 + 512],
                        start=False, stop=True), waits=[t_brow])
                    t_res = sc.emit("dve", lambda v, ps=ps, tt=tt, hf=hf: v.tensor_add(
                        h.s[:, tt, hf * 512:hf * 512 + 512],
                        h.s[:, tt, hf * 512:hf * 512 + 512], ps[:]), waits=[mm])
                    psBC_free[u] = t_res
                    s_tok[tt] = t_res
            wout_done = mm

        ph = {"pe": ("pe", sc.cnt["pe"]), "dve": ("dve", sc.cnt["dve"]),
              "act": ("act", sc.cnt["act"])}

        if n_sub >= 2 and not SKIP12:
            ph = _emit_ffn(sc, h, dict(
                fwq=d_fwq, fwk=d_fwk, fwv=d_fwv, fwo=d_fwo,
                fgate=d_fgate, fexp=d_fexp), s_tok, ph)
        if n_sub >= 3:
            ph = _emit_attn_b(sc, h, dict(
                bwq=d_bwq, bwk=d_bwk, bwv=d_bwv, bwo=d_bwo, bww1=d_bww1,
                bww2=d_bww2, bwbf=d_bwbf, bdelta=d_bdelta), s_tok, ph)
        if n_sub >= 4:
            ph = _emit_ffn(sc, h, dict(
                fwq=d_fwq, fwk=d_fwk, fwv=d_fwv, fwo=d_fwo,
                fgate=d_fgate, fexp=d_fexp), s_tok, ph)

        tok_s1 = sc.barrier_tokens()

        t_store = sc.emit("sync", lambda y: y.dma_start(
            out=d_out.rearrange("(tt p) c -> p tt c", p=128), in_=h.s[:]),
            waits=tok_s1, dma="s0")
        sc.emit("sync", lambda y: y.wait_ge(sems["s0"], sc.cnt["s0"]), inc=False)

        sc.run(blk, sems)
    return nc


def prep_inputs(inputs, b):
    asc = np.ascontiguousarray
    wouta = np.concatenate([inputs["L0_attn_a_w_out"],
                            inputs["L0_attn_a_b_out"][None, :]], 0)
    m = {
        "x": asc(inputs["L0_attn_a_x"][b]),
        "wqkv": asc(inputs["L0_attn_a_w_qkv"]),
        "bqkv": asc(inputs["L0_attn_a_b_qkv"].reshape(24, 128).T),
        "wouta": asc(wouta),
        "fwq": asc(inputs["L0_ffn_wq"].T),
        "fwk": asc(inputs["L0_ffn_wk"].T),
        "fwv": asc(inputs["L0_ffn_wv"].T),
        "fwo": asc(inputs["L0_ffn_wo"].T),
        "fgate": asc(inputs["L0_ffn_gate_w"].T),
        "fexp": asc(np.transpose(inputs["L0_ffn_expert_w"], (0, 2, 1))),
        "fnw": asc(np.concatenate(
            [inputs["L0_ffn_attn_norm_w"].reshape(8, 128).T,
             inputs["L0_ffn_ffn_norm_w"].reshape(8, 128).T], 1)),
        "bwq": asc(inputs["L0_attn_b_Wq"]),
        "bwk": asc(inputs["L0_attn_b_Wk"]),
        "bwv": asc(inputs["L0_attn_b_Wv"]),
        "bwo": asc(inputs["L0_attn_b_Wo"]),
        "bww1": asc(inputs["L0_attn_b_Ww1"]),
        "bww2": asc(inputs["L0_attn_b_Ww2"]),
        "bwbf": asc(np.concatenate([inputs["L0_attn_b_Wb"],
                                    inputs["L0_attn_b_Wf"]], 1)),
        "bdelta": asc(inputs["L0_attn_b_delta"][None, :]),
    }
    return m


def kernel(**inputs):
    nc = build_nc(N_SUB)
    in_maps = [prep_inputs(inputs, b) for b in range(NCORES)]
    res = run_bass_kernel_spmd(nc, in_maps, core_ids=list(range(NCORES)))
    out = np.stack([res.results[b]["out"] for b in range(NCORES)], 0)
    return out.astype(np.float32)
